# version 05
# baseline (speedup 1.0000x reference)
import numpy as np
from contextlib import ExitStack

import concourse.bass as bass
import concourse.mybir as mybir
from concourse.bass_utils import run_bass_kernel_spmd

F32 = mybir.dt.float32
BF16 = mybir.dt.bfloat16
I32 = mybir.dt.int32
AF = mybir.ActivationFunctionType
ALU = mybir.AluOpType
AX = mybir.AxisListType


class Cfg:
    def __init__(self, D, L, H, HID, topu, n_cores, bis):
        self.D, self.L, self.H, self.HID = D, L, H, HID
        self.topu, self.n_cores, self.bis = topu, n_cores, bis
        self.DC = D // 128
        self.KC = L // 128
        self.TT = 256
        self.NT = L // self.TT
        self.PASSES = max(1, H // 8)
        self.HPP = H // self.PASSES
        self.C4 = self.HPP // 2
        self.DVP = self.HPP * 64
        self.half = L // 2
        self.QG = self.half // 512
        self.HC = HID // 128
        self.TC = L // 1024
        self.VST = self.DVP + self.HPP


def full_cfg():
    return Cfg(D=1024, L=8192, H=16, HID=3072, topu=1638, n_cores=8, bis=34)


class KB:
    ENGS = ("sync", "tensor", "vector", "scalar", "gpsimd")

    def __init__(self, nc, st):
        self.nc = nc
        self.st = st
        self.ops = {e: [] for e in self.ENGS}
        self.sems = {}

    def _ensure(self, name):
        if name not in self.sems:
            h = self.st.enter_context(self.nc.semaphore(name))
            self.sems[name] = [h, 0]

    def tick(self, name, n=1):
        self._ensure(name)
        s = self.sems[name]
        s[1] += n
        return s[1]

    def val(self, name):
        self._ensure(name)
        return self.sems[name][1]

    def h(self, name):
        self._ensure(name)
        return self.sems[name][0]

    def op(self, eng, make, incs=(), waits=()):
        waits = list(waits)
        incs = list(incs)
        if eng == "vector":
            waits.append(("v_dv", self.val("v_dv")))
            incs.append(("v_dv", 1))
        elif eng == "scalar":
            waits.append(("a_sc", self.val("a_sc")))
            incs.append(("a_sc", 1))
        elif eng == "gpsimd":
            waits.append(("g_gp", self.val("g_gp")))
        wl = [(self.h(s), v) for (s, v) in waits if v > 0]
        tk = [(self.h(s), n) for (s, n) in incs]
        for (s, n) in incs:
            self.tick(s, n)

        def fn(e, make=make, wl=wl, tk=tk):
            for hdl, v in wl:
                e.wait_ge(hdl, v)
            ins = make(e)
            for hdl, n in tk:
                ins.then_inc(hdl, n)
        self.ops[eng].append(fn)

    def dma(self, eng, make, sem, waits=()):
        self.op(eng, make, incs=[(sem, 16)], waits=waits)

    def wait(self, eng, waits):
        wl = [(self.h(s), v) for (s, v) in waits if v > 0]

        def fn(e, wl=wl):
            for hdl, v in wl:
                e.wait_ge(hdl, v)
        self.ops[eng].append(fn)


DBG = False


def build_nc(c: Cfg):
    nc = bass.Bass("TRN2", target_bir_lowering=False)
    D, L, H, HID = c.D, c.L, c.H, c.HID
    DC, KC, TT, NT = c.DC, c.KC, c.TT, c.NT
    C4, QG, HC, TC = c.C4, c.QG, c.HC, c.TC
    half, HPP, DVP, VST = c.half, c.HPP, c.DVP, c.VST

    xT = nc.dram_tensor("xT", [D, L], F32, kind="ExternalInput")
    wqT = nc.dram_tensor("wqT", [D, D], F32, kind="ExternalInput")
    wkT = nc.dram_tensor("wkT", [D, D], F32, kind="ExternalInput")
    wvT = nc.dram_tensor("wvT", [D, D], F32, kind="ExternalInput")
    woT = nc.dram_tensor("woT", [D, D], F32, kind="ExternalInput")
    bqc = nc.dram_tensor("bqc", [128, DC], F32, kind="ExternalInput")
    bkc = nc.dram_tensor("bkc", [128, DC], F32, kind="ExternalInput")
    boc = nc.dram_tensor("boc", [128, DC], F32, kind="ExternalInput")
    bvr = nc.dram_tensor("bvr", [1, D], F32, kind="ExternalInput")
    wgT = nc.dram_tensor("wgT", [D, HID], I32, kind="ExternalInput")
    wuT = nc.dram_tensor("wuT", [D, HID], I32, kind="ExternalInput")
    wdT = nc.dram_tensor("wdT", [HID, D], I32, kind="ExternalInput")
    PAIRSEL = nc.dram_tensor("PAIRSEL", [128, 2], F32, kind="ExternalInput")
    GSEL8 = nc.dram_tensor("GSEL8", [64, 8], F32, kind="ExternalInput")
    GBC8 = nc.dram_tensor("GBC8", [8, 64], F32, kind="ExternalInput")
    IDENT8 = nc.dram_tensor("IDENT8", [8, 8], F32, kind="ExternalInput")
    ONES8_128 = nc.dram_tensor("ONES8_128", [8, 128], F32, kind="ExternalInput")
    ONES1_64 = nc.dram_tensor("ONES1_64", [1, 64], BF16, kind="ExternalInput")
    ONES2_128 = nc.dram_tensor("ONES2_128", [2, 128], BF16, kind="ExternalInput")
    ALLONES = nc.dram_tensor("ALLONES", [128, 128], BF16, kind="ExternalInput")

    yout = nc.dram_tensor("yout", [D, half], F32, kind="ExternalOutput")

    _k = dict(kind="ExternalOutput") if DBG else {}
    QD = nc.dram_tensor("QD", [D, L], BF16, **_k)
    ATTD = nc.dram_tensor("ATTD", [D, half], BF16, **_k)
    LN1D = nc.dram_tensor("LN1D", [D, half], F32, **_k)
    LN1B = nc.dram_tensor("LN1B", [D, half], BF16, **_k)
    MASKD = nc.dram_tensor("MASKD", [H, half], F32, **_k)
    if DBG:
        DBGG = nc.dram_tensor("DBGG", [HID, 512], BF16, kind="ExternalOutput")
        DBGU = nc.dram_tensor("DBGU", [HID, 512], BF16, kind="ExternalOutput")
        DBGH = nc.dram_tensor("DBGH", [HID, 512], BF16, kind="ExternalOutput")
        DBGL = nc.dram_tensor("DBGL", [D, 512], F32, kind="ExternalOutput")
        DBGM = nc.dram_tensor("DBGM", [128, 512], F32, kind="ExternalOutput")
        DBGR = nc.dram_tensor("DBGR", [128, 512], F32, kind="ExternalOutput")

    st = ExitStack()
    with st:
        kb = KB(nc, st)
        SB = lambda n, shp, dt: st.enter_context(nc.sbuf_tensor(n, shp, dt))

        bqcol = SB("bqcol", [128, DC], F32)
        bkcol = SB("bkcol", [128, DC], F32)
        bocol = SB("bocol", [128, DC], F32)
        pairsel = SB("pairsel", [128, 2], F32)
        gsel8 = SB("gsel8", [64, 8], F32)
        gbc8 = SB("gbc8", [8, 64], F32)
        ident8 = SB("ident8", [8, 8], F32)
        ones8_128 = SB("ones8_128", [8, 128], F32)
        ones1_64 = SB("ones1_64", [1, 64], BF16)
        ones2_128 = SB("ones2_128", [2, 128], BF16)
        allones = SB("allones", [128, 128], BF16)
        constv = SB("constv", [128, 4], F32)
        nc.const_aps.register(F32, 1e-5, constv[:, 0:1])
        nc.const_aps.register(F32, 0.125, constv[:, 1:2])
        nc.const_aps.register(F32, 1.0 / 127.0, constv[:, 2:3])
        nc.const_aps.register(F32, -0.5, constv[:, 3:4])
        kb.op("vector", lambda e: e.memset(constv[:, 0:1], 1e-5))
        kb.op("vector", lambda e: e.memset(constv[:, 1:2], 0.125))
        kb.op("vector", lambda e: e.memset(constv[:, 2:3], 1.0 / 127.0))
        kb.op("vector", lambda e: e.memset(constv[:, 3:4], -0.5))
        constv_v = ("v_dv", kb.val("v_dv"))

        for dst, src in [(bqcol, bqc), (bkcol, bkc), (bocol, boc),
                         (pairsel, PAIRSEL), (gsel8, GSEL8), (gbc8, GBC8),
                         (ident8, IDENT8), (ones8_128, ONES8_128),
                         (ones1_64, ONES1_64), (ones2_128, ONES2_128),
                         (allones, ALLONES)]:
            kb.dma("sync", lambda e, d=dst, s=src: e.dma_start(out=d[:, :], in_=s.ap()),
                   "h_cst")
        cst = ("h_cst", kb.val("h_cst"))

        qf_used_pe = [0]
        q_dma_tot = [0]

        st_att = ExitStack()
        SBA = lambda n, shp, dt: st_att.enter_context(nc.sbuf_tensor(n, shp, dt))
        KF = [SBA(f"KF{i}", [128, L], BF16) for i in range(C4)]
        VSB = SBA("VSB", [128, KC * VST], BF16)
        norms2 = SBA("norms2", [64, 1024], F32)
        bisw = SBA("bisw", [64, 1024], F32)
        bvhl = SBA("bvhl", [2, D], BF16)
        tlo = SBA("tlo", [64, 1], F32)
        thi = SBA("thi", [64, 1], F32)
        tmid = SBA("tmid", [64, 1], F32)
        tsel = SBA("tsel", [64, 1], I32)
        ttmp = SBA("ttmp", [64, 1], F32)
        bpart = SBA("bpart", [64, 1], F32)
        cnt8 = SBA("cnt8", [8, 1], F32)
        rowmax = SBA("rowmax", [64, 1], F32)
        hm8 = SBA("hm8", [8, 8], F32)
        mq8 = SBA("mq8", [8, 1], F32)
        mk8 = SBA("mk8", [8, 1], F32)
        c8 = SBA("c8", [8, 1], F32)
        cdiag = SBA("cdiag", [8, 8], F32)
        negC = SBA("negC", [128, 8], F32)
        maxk2 = SBA("maxk2", [2, C4], F32)
        kntmp = SBA("kntmp", [2, 1], F32)
        n2sb = SBA("n2sb", [2, TT], F32)
        QF = [SBA(f"QF{i}", [128, 512], BF16) for i in range(2)]
        expT = [SBA(f"expT{i}", [128, 512], BF16) for i in range(2)]
        mrow = [SBA(f"mrow{i}", [1, 512], BF16) for i in range(2)]
        rd_f = SBA("rd_f", [1, 512], F32)
        rdm_b = SBA("rdm_b", [1, 512], BF16)
        bc_sb = SBA("bc_sb", [64, 512], F32)
        att_b = SBA("att_b", [64, 512], BF16)

        with ExitStack() as stbv:
            bv_f = stbv.enter_context(nc.sbuf_tensor("bv_f", [1, D], F32))
            bv_hi = stbv.enter_context(nc.sbuf_tensor("bv_hi", [1, D], BF16))
            bv_lo = stbv.enter_context(nc.sbuf_tensor("bv_lo", [1, D], BF16))
            kb.dma("sync", lambda e: e.dma_start(out=bv_f[:, :], in_=bvr.ap()), "h_cst")
            cst_bv = ("h_cst", kb.val("h_cst"))
            kb.op("vector", lambda e: e.tensor_copy(bv_hi[:, :], bv_f[:, :]),
                  waits=[cst_bv])
            kb.op("vector", lambda e: e.scalar_tensor_tensor(
                bv_lo[:, :], bv_f[:, :], 1.0, bv_hi[:, :], ALU.mult, ALU.subtract))
            vbv = ("v_dv", kb.val("v_dv"))
            kb.dma("sync", lambda e: e.dma_start(out=bvhl[0:1, :], in_=bv_hi[:, :]),
                   "h_cst", waits=[vbv])
            kb.dma("sync", lambda e: e.dma_start(out=bvhl[1:2, :], in_=bv_lo[:, :]),
                   "h_cst")
            bvhl_done = ("h_cst", kb.val("h_cst"))
        vap = VSB[:, :]
        kb.op("vector", lambda e: e.memset(
            bass.AP(vap.tensor, vap.offset + 64, [[vap.ap[0][0], 128], [65, KC * HPP]]),
            1.0))
        vones = ("v_dv", kb.val("v_dv"))

        for p in range(c.PASSES):
            pass_t0 = kb.val("t_pe")
            pass_a0 = kb.val("a_sc")
            pass_vv0 = kb.val("v_dv")
            pass_v0 = ("v_dv", kb.val("v_dv"))
            pass_msk0 = ("h_msk", kb.val("h_msk"))
            pass_mx0 = ("h_mx", kb.val("h_mx"))
            pass_asc0 = ("a_sc", kb.val("a_sc"))
            with ExitStack() as stp:
                SBp = lambda n, shp, dt: stp.enter_context(nc.sbuf_tensor(n, shp, dt))

                with ExitStack() as stq:
                    SBq = lambda n, shp, dt: stq.enter_context(nc.sbuf_tensor(f"{n}_{p}", shp, dt))
                    PSq = lambda n, shp, dt: stq.enter_context(nc.psum_tensor(f"{n}_{p}", shp, dt))
                    wq_f = SBq("wq_f", [128, DVP], F32)
                    wq_hi = SBq("wq_hi", [128, DC, DVP], BF16)
                    wq_lo = SBq("wq_lo", [128, DC, DVP], BF16)
                    xf = [SBq(f"xf{i}", [128, DC, TT], F32) for i in range(2)]
                    xhi = SBq("xhi", [128, DC, TT], BF16)
                    xlo = SBq("xlo", [128, DC, TT], BF16)
                    qf32 = SBq("qf32", [128, TT], F32)
                    qsq = SBq("qsq", [128, TT], F32)
                    qbf = SBq("qbf", [128, TT], BF16)
                    qps = [PSq(f"qps{i}", [128, 512], F32) for i in range(2)]
                    n2ps = PSq("n2ps", [128, 512], F32)

                    wq_stage_v = ("v_dv", 0)
                    for dcq in range(DC):
                        kb.dma("gpsimd", lambda e, p=p, dcq=dcq: e.dma_start(
                            out=wq_f[:, :],
                            in_=bass.AP(wqT.ap().tensor, dcq * 128 * D + p * DVP,
                                        [[D, 128], [1, DVP]])), "g_w",
                            waits=[wq_stage_v] +
                                  ([bvhl_done, ("t_pe", pass_t0), ("a_sc", pass_a0),
                                    ("v_dv", pass_vv0)] if dcq == 0 else []))
                        gw = ("g_w", kb.val("g_w"))
                        kb.op("vector", lambda e, dcq=dcq: e.tensor_copy(
                            wq_hi[:, dcq, :], wq_f[:, :]), waits=[gw])
                        kb.op("vector", lambda e, dcq=dcq: e.scalar_tensor_tensor(
                            wq_lo[:, dcq, :], wq_f[:, :], 1.0, wq_hi[:, dcq, :],
                            ALU.mult, ALU.subtract))
                        wq_stage_v = ("v_dv", kb.val("v_dv"))
                    wq_split_v = ("v_dv", kb.val("v_dv"))

                    qps_free = {0: ("a_sc", 0), 1: ("a_sc", 0)}
                    x_split_v = ("v_dv", 0)
                    n2_evac = ("v_dv", 0)
                    n2_dma = ("h_n2", 0)
                    qbf_v = ("v_dv", 0)
                    qmm_last = 0
                    for tg in range(NT):
                        xb = xf[tg % 2]
                        xs = f"g_x{tg % 2}"
                        w = [] if tg < 2 else [x_split_v]
                        kb.dma("gpsimd", lambda e, tg=tg, xb=xb: e.dma_start(
                            out=xb[:, :, :],
                            in_=bass.AP(xT.ap().tensor, tg * TT,
                                        [[L, 128], [128 * L, DC], [1, TT]])),
                            xs, waits=w)
                        xw = (xs, kb.val(xs))
                        wv = [xw]
                        if tg >= 1:
                            wv.append(("t_pe", qmm_last))
                        for dcq in range(DC):
                            kb.op("vector", lambda e, dcq=dcq, xb=xb: e.tensor_copy(
                                xhi[:, dcq, :], xb[:, dcq, :]),
                                waits=(wv if dcq == 0 else []))
                            kb.op("vector", lambda e, dcq=dcq, xb=xb: e.scalar_tensor_tensor(
                                xlo[:, dcq, :], xb[:, dcq, :], 1.0, xhi[:, dcq, :],
                                ALU.mult, ALU.subtract))
                        x_split_v = ("v_dv", kb.val("v_dv"))

                        for c4 in range(C4):
                            bank = (tg * C4 + c4) % 2
                            ps = qps[bank]
                            w0 = [x_split_v, wq_split_v, qps_free[bank]]
                            nmm = 0
                            for (wa, xa) in ((wq_hi, xhi), (wq_hi, xlo), (wq_lo, xhi)):
                                for dcq in range(DC):
                                    nmm += 1
                                    kb.op("tensor",
                                          lambda e, wa=wa, xa=xa, dcq=dcq, c4=c4, ps=ps,
                                          s=(nmm == 1), f=(nmm == 3 * DC): e.matmul(
                                              ps[:, 0:TT],
                                              wa[:, dcq, c4 * 128:(c4 + 1) * 128],
                                              xa[:, dcq, :], start=s, stop=f),
                                          waits=(w0 if nmm == 1 else []),
                                          incs=([("t_pe", 1)] if nmm == 3 * DC else []))
                            qmm_last = kb.val("t_pe")
                            qmm = ("t_pe", qmm_last)
                            kb.op("scalar", lambda e, ps=ps, c4=c4, p=p: e.activation(
                                out=qf32[:, :], in_=ps[:, 0:TT], func=AF.Identity,
                                bias=bqcol[:, p * C4 + c4:p * C4 + c4 + 1]),
                                waits=[qmm, n2_evac, qbf_v, ("h_q", kb.val("h_q"))])
                            qps_free[bank] = ("a_sc", kb.val("a_sc"))
                            kb.op("scalar", lambda e: e.activation(
                                out=qsq[:, :], in_=qf32[:, :], func=AF.Square))
                            sq_a = ("a_sc", kb.val("a_sc"))
                            kb.op("tensor", lambda e: e.matmul(
                                n2ps[0:2, 0:TT], pairsel[:, :], qsq[:, :],
                                start=True, stop=True),
                                waits=[sq_a, n2_evac, cst], incs=[("t_pe", 1)])
                            nmmt = ("t_pe", kb.val("t_pe"))
                            kb.op("vector", lambda e: e.tensor_copy(
                                n2sb[:, :], n2ps[0:2, 0:TT]),
                                waits=[nmmt, n2_dma])
                            n2_evac = ("v_dv", kb.val("v_dv"))
                            tc = (tg * TT) // 1024
                            co = (tg * TT) % 1024
                            np2 = norms2[:, :]
                            kb.dma("sync", lambda e, c4=c4, tc=tc, co=co, np2=np2: e.dma_start(
                                out=bass.AP(np2.tensor,
                                            np2.offset + (2 * c4 * 8 + tc) * np2.ap[0][0] + co,
                                            [[8 * np2.ap[0][0], 2], [1, TT]]),
                                in_=n2sb[:, :]), "h_n2",
                                waits=[n2_evac, pass_v0])
                            n2_dma = ("h_n2", kb.val("h_n2"))
                            kb.op("vector", lambda e: e.tensor_copy(qbf[:, :], qf32[:, :]),
                                  waits=[sq_a, ("h_q", kb.val("h_q"))])
                            qbf_v = ("v_dv", kb.val("v_dv"))
                            kb.dma("sync", lambda e, c4=c4, tg=tg, p=p: e.dma_start(
                                out=bass.AP(QD.ap().tensor,
                                            (p * DVP + c4 * 128) * L + tg * TT,
                                            [[L, 128], [1, TT]]),
                                in_=qbf[:, :]), "h_q", waits=[qbf_v])
                    q_dma_tot[0] = kb.val("h_q")
                q_all = [("h_q", kb.val("h_q")), ("h_n2", kb.val("h_n2"))]
                q_phase_done = q_all + [("t_pe", kb.val("t_pe")),
                                        ("a_sc", kb.val("a_sc")),
                                        ("v_dv", kb.val("v_dv"))]

                kb.op("vector", lambda e: e.tensor_reduce(
                    rowmax[:, :], norms2[:, :], AX.X, ALU.max), waits=q_all)
                vrm = ("v_dv", kb.val("v_dv"))
                rm = rowmax[:, :]
                kb.dma("sync", lambda e, rm=rm: e.dma_start(
                    out=hm8[:, :],
                    in_=bass.AP(rm.tensor, rm.offset, [[rm.ap[0][0], 64], [1, 1]])),
                    "h_mx", waits=[vrm])
                hmx = ("h_mx", kb.val("h_mx"))
                kb.op("vector", lambda e: e.tensor_reduce(
                    mq8[:, :], hm8[:, :], AX.X, ALU.max), waits=[hmx])
                kb.op("vector", lambda e: e.tensor_scalar(
                    cnt8[:, :], mq8[:, :], 1.01, None, ALU.mult))
                v_hi0 = ("v_dv", kb.val("v_dv"))

                with ExitStack() as stk:
                    SBk = lambda n, shp, dt: stk.enter_context(nc.sbuf_tensor(f"{n}_{p}", shp, dt))
                    PSk = lambda n, shp, dt: stk.enter_context(nc.psum_tensor(f"{n}_{p}", shp, dt))
                    wk_sb = SBk("wk_sb", [128, DC, DVP], BF16)
                    wv_sb = SBk("wv_sb", [128, DC, DVP], BF16)
                    xkb = [SBk(f"xkb{i}", [128, DC, TT], BF16) for i in range(2)]
                    ksq = SBk("ksq", [128, TT], F32)
                    kps = [PSk(f"kps{i}", [128, 512], F32) for i in range(2)]
                    vps = [PSk(f"vps{i}", [128, 512], F32) for i in range(2)]
                    kn_ps = PSk("kn_ps", [128, 512], F32)
                    cnt_ps = PSk("cnt_ps", [128, 512], F32)
                    bc64_ps = PSk("bc64_ps", [128, 512], F32)
                    negc_ps = PSk("negc_ps", [128, 512], F32)

                    kb.dma("gpsimd", lambda e, p=p: e.dma_start(
                        out=wk_sb[:, :, :],
                        in_=bass.AP(wkT.ap().tensor, p * DVP,
                                    [[D, 128], [128 * D, DC], [1, DVP]])), "g_w",
                           waits=q_phase_done)
                    kb.dma("gpsimd", lambda e, p=p: e.dma_start(
                        out=wv_sb[:, :, :],
                        in_=bass.AP(wvT.ap().tensor, p * DVP,
                                    [[D, 128], [128 * D, DC], [1, DVP]])), "g_w")
                    gw2 = ("g_w", kb.val("g_w"))

                    kb.op("vector", lambda e: e.memset(maxk2[:, :], 0.0),
                          waits=[pass_mx0])
                    vmk0 = ("v_dv", kb.val("v_dv"))

                    bis_iter = [0]

                    def emit_bis_iter():
                        it = bis_iter[0]
                        if it >= c.bis + 1:
                            return
                        bis_iter[0] += 1
                        if it == 0:
                            kb.op("tensor", lambda e: e.matmul(
                                bc64_ps[0:64, 0:1], gbc8[:, :], cnt8[:, :],
                                start=True, stop=True),
                                waits=[v_hi0, cst], incs=[("t_pe", 1)])
                            tb = ("t_pe", kb.val("t_pe"))
                            kb.op("vector", lambda e: e.tensor_copy(
                                thi[:, :], bc64_ps[0:64, 0:1]), waits=[tb])
                            kb.op("vector", lambda e: e.memset(tlo[:, :], 0.0))
                            kb.op("vector", lambda e: e.tensor_scalar(
                                tmid[:, :], thi[:, :], 0.5, None, ALU.mult))
                            return
                        kb.op("vector", lambda e: e.tensor_scalar(
                            bisw[:, :], norms2[:, :], tmid[:, :], None, ALU.is_gt),
                            waits=[pass_msk0])
                        kb.op("vector", lambda e: e.tensor_reduce(
                            bpart[:, :], bisw[:, :], AX.X, ALU.add))
                        vp_ = ("v_dv", kb.val("v_dv"))
                        kb.op("tensor", lambda e: e.matmul(
                            cnt_ps[0:8, 0:1], gsel8[:, :], bpart[:, :],
                            start=True, stop=True),
                            waits=[vp_, cst], incs=[("t_pe", 1)])
                        t1 = ("t_pe", kb.val("t_pe"))
                        kb.op("vector", lambda e: e.tensor_copy(
                            cnt8[:, :], cnt_ps[0:8, 0:1]), waits=[t1])
                        v1 = ("v_dv", kb.val("v_dv"))
                        kb.op("tensor", lambda e: e.matmul(
                            bc64_ps[0:64, 0:1], gbc8[:, :], cnt8[:, :],
                            start=True, stop=True),
                            waits=[v1], incs=[("t_pe", 1)])
                        t2 = ("t_pe", kb.val("t_pe"))
                        kb.op("vector", lambda e: e.tensor_scalar(
                            tsel[:, :], bc64_ps[0:64, 0:1], float(c.topu), None,
                            ALU.is_ge), waits=[t2])
                        kb.op("vector", lambda e: e.select(
                            tlo[:, :], tsel[:, :], tmid[:, :], tlo[:, :]))
                        kb.op("vector", lambda e: e.select(
                            thi[:, :], tsel[:, :], thi[:, :], tmid[:, :]))
                        kb.op("vector", lambda e: e.tensor_tensor(
                            ttmp[:, :], tlo[:, :], thi[:, :], ALU.add))
                        kb.op("vector", lambda e: e.tensor_scalar(
                            tmid[:, :], ttmp[:, :], 0.5, None, ALU.mult))

                    k_evac = {}
                    kn_evac = ("v_dv", 0)
                    kps_free = {0: ("a_sc", 0), 1: ("a_sc", 0)}
                    vps_free = {0: ("v_dv", 0), 1: ("v_dv", 0)}
                    x_loaded = {}
                    for tg in range(NT):
                        xb = xkb[tg % 2]
                        xs = f"g_kx{tg % 2}"
                        w = [gw2] if tg < 2 else [("t_pe", x_loaded[tg - 2])]
                        kb.dma("gpsimd", lambda e, tg=tg, xb=xb: e.dma_start(
                            out=xb[:, :, :],
                            in_=bass.AP(xT.ap().tensor, tg * TT,
                                        [[L, 128], [128 * L, DC], [1, TT]])),
                            xs, waits=w)
                        xw = (xs, kb.val(xs))
                        for c4 in range(C4):
                            bank = (tg * C4 + c4) % 2
                            ps = kps[bank]
                            w0 = [xw, gw2, kps_free[bank]]
                            for dcq in range(DC):
                                kb.op("tensor", lambda e, dcq=dcq, c4=c4, ps=ps, xb=xb:
                                      e.matmul(
                                          ps[:, 0:TT],
                                          wk_sb[:, dcq, c4 * 128:(c4 + 1) * 128],
                                          xb[:, dcq, :], start=(dcq == 0),
                                          stop=(dcq == DC - 1)),
                                      waits=(w0 if dcq == 0 else []),
                                      incs=([("t_pe", 1)] if dcq == DC - 1 else []))
                            kmm = ("t_pe", kb.val("t_pe"))
                            kb.op("scalar", lambda e, ps=ps, c4=c4, tg=tg, p=p: e.activation(
                                out=KF[c4][:, tg * TT:(tg + 1) * TT], in_=ps[:, 0:TT],
                                func=AF.Identity,
                                bias=bkcol[:, p * C4 + c4:p * C4 + c4 + 1]),
                                waits=[kmm])
                            k_evac[(c4, tg)] = ("a_sc", kb.val("a_sc"))
                            kb.op("scalar", lambda e, ps=ps: e.activation(
                                out=ksq[:, :], in_=ps[:, 0:TT], func=AF.Square))
                            ks_a = ("a_sc", kb.val("a_sc"))
                            kps_free[bank] = ks_a
                            kb.op("tensor", lambda e: e.matmul(
                                kn_ps[0:2, 0:TT], pairsel[:, :], ksq[:, :],
                                start=True, stop=True),
                                waits=[ks_a, kn_evac], incs=[("t_pe", 1)])
                            knt = ("t_pe", kb.val("t_pe"))
                            kb.op("vector", lambda e: e.tensor_reduce(
                                kntmp[:, :], kn_ps[0:2, 0:TT], AX.X, ALU.max),
                                waits=[knt, vmk0])
                            kn_evac = ("v_dv", kb.val("v_dv"))
                            kb.op("vector", lambda e, c4=c4: e.tensor_tensor(
                                maxk2[:, c4:c4 + 1], maxk2[:, c4:c4 + 1],
                                kntmp[:, :], ALU.max))
                        for ts_ in range(TT // 128):
                            kt = (tg * TT) // 128 + ts_
                            bank = kt % 2
                            ps = vps[bank]
                            w0 = [xw, gw2, vps_free[bank], bvhl_done]
                            for dcq in range(DC):
                                kb.op("tensor", lambda e, dcq=dcq, ts_=ts_, ps=ps, xb=xb:
                                      e.matmul(
                                          ps[:, :], xb[:, dcq, ts_ * 128:(ts_ + 1) * 128],
                                          wv_sb[:, dcq, :], start=(dcq == 0), stop=False),
                                      waits=(w0 if dcq == 0 else []))
                            kb.op("tensor", lambda e, ps=ps, p=p: e.matmul(
                                ps[:, :], ones2_128[:, :], bvhl[:, p * DVP:(p + 1) * DVP],
                                start=False, stop=True), incs=[("t_pe", 1)])
                            vmm = ("t_pe", kb.val("t_pe"))
                            kb.op("vector", lambda e, ps=ps, kt=kt, vap=vap: e.tensor_copy(
                                bass.AP(vap.tensor, vap.offset + kt * VST,
                                        [[vap.ap[0][0], 128], [65, HPP], [1, 64]]),
                                ps[:, :].rearrange("p (h d) -> p h d", d=64)),
                                waits=[vmm, vones])
                            vps_free[bank] = ("v_dv", kb.val("v_dv"))
                        x_loaded[tg] = kb.val("t_pe")
                        emit_bis_iter()
                    while bis_iter[0] < c.bis + 1:
                        emit_bis_iter()
                    kv_done = [("a_sc", kb.val("a_sc")), ("v_dv", kb.val("v_dv")),
                               ("t_pe", kb.val("t_pe"))]

                    kb.op("vector", lambda e: e.tensor_scalar(
                        bisw[:, :], norms2[:, :], tlo[:, :], None, ALU.is_gt))
                    cmpv = ("v_dv", kb.val("v_dv"))
                    m8 = mk8[:, :]
                    for j in range(2):
                        kb.dma("sync", lambda e, j=j, m8=m8: e.dma_start(
                            out=bass.AP(m8.tensor, m8.offset + j * m8.ap[0][0],
                                        [[2 * m8.ap[0][0], C4], [1, 1]]),
                            in_=maxk2[j:j + 1, :]), "h_mx",
                            waits=(kv_done if j == 0 else []))
                    hmk = ("h_mx", kb.val("h_mx"))
                    kb.op("vector", lambda e: e.tensor_tensor(
                        c8[:, :], mq8[:, :], mk8[:, :], ALU.add), waits=[hmk])
                    kb.op("vector", lambda e: e.tensor_scalar(
                        c8[:, :], c8[:, :], -1.0 / 16.0, None, ALU.mult))
                    kb.op("vector", lambda e: e.tensor_scalar(
                        cdiag[:, :], ident8[:, :], c8[:, :], None, ALU.mult),
                        waits=[cst])
                    vcd = ("v_dv", kb.val("v_dv"))
                    kb.op("tensor", lambda e: e.matmul(
                        negc_ps[0:128, 0:8], ones8_128[:, :], cdiag[:, :],
                        start=True, stop=True), waits=[vcd], incs=[("t_pe", 1)])
                    tnc = ("t_pe", kb.val("t_pe"))
                    kb.op("vector", lambda e: e.tensor_copy(
                        negC[:, :], negc_ps[0:128, 0:8]), waits=[tnc, pass_asc0])
                    negc_v = ("v_dv", kb.val("v_dv"))
                    bw = bisw[:, :]
                    for tci in range(TC // 2):
                        kb.dma("sync", lambda e, tci=tci, p=p, bw=bw: e.dma_start(
                            out=bass.AP(MASKD.ap().tensor,
                                        (p * HPP) * half + tci * 1024,
                                        [[half, HPP], [1, 1024]]),
                            in_=bass.AP(bw.tensor, bw.offset + tci * bw.ap[0][0],
                                        [[8 * bw.ap[0][0], HPP], [1, 1024]])),
                            "h_msk", waits=[cmpv])
                    mask_done = ("h_msk", kb.val("h_msk"))

                with ExitStack() as sta:
                    PSa = lambda n, shp, dt: sta.enter_context(nc.psum_tensor(f"{n}_{p}", shp, dt))
                    psA = [PSa(f"psA{i}", [128, 512], F32) for i in range(4)]
                    psO = [PSa(f"psO{i}", [128, 512], F32) for i in range(2)]
                    bc_ps = PSa("bc_ps", [128, 512], F32)
                    psA_free = {i: ("a_sc", 0) for i in range(4)}
                    psO_free = {i: ("v_dv", 0) for i in range(2)}
                    bc_free = ("v_dv", 0)
                    expT_free = {i: ("t_pe", 0) for i in range(2)}
                    att_dma = ("h_att", 0)
                    mrow_v = ("v_dv", 0)
                    for c4 in range(C4):
                        for qg in range(QG):
                            u = c4 * QG + qg
                            qfb = QF[u % 2]
                            kb.dma("sync", lambda e, c4=c4, qg=qg, p=p, qfb=qfb: e.dma_start(
                                out=qfb[:, :],
                                in_=bass.AP(QD.ap().tensor,
                                            (p * DVP + c4 * 128) * L + qg * 512,
                                            [[L, 128], [1, 512]])), "h_qf",
                                waits=[q_all[0], ("t_pe", qf_used_pe[0])])
                            qf_w = ("h_qf", kb.val("h_qf"))
                            for j in range(2):
                                kb.dma("gpsimd", lambda e, c4=c4, qg=qg, j=j, p=p: e.dma_start(
                                    out=mrow[j][:, :],
                                    in_=bass.AP(MASKD.ap().tensor,
                                                (p * HPP + 2 * c4 + j) * half + qg * 512,
                                                [[half, 1], [1, 512]])), "g_mr",
                                    waits=([mask_done, mrow_v] if j == 0 else []))
                            mr_w = ("g_mr", kb.val("g_mr"))
                            for kt in range(KC):
                                par = kt % 2
                                pe_, po_ = psA[par * 2], psA[par * 2 + 1]
                                eb = expT[par]
                                w_qk = [qf_w, psA_free[par * 2], psA_free[par * 2 + 1]]
                                if u == 0 and kt == 0:
                                    w_qk = w_qk + kv_done
                                kb.op("tensor", lambda e, c4=c4, kt=kt, pe_=pe_, qfb=qfb:
                                      e.matmul(
                                          pe_[:, :], KF[c4][0:64, kt * 128:(kt + 1) * 128],
                                          qfb[0:64, :], start=True, stop=True,
                                          tile_position=(0, 0)), waits=w_qk)
                                kb.op("tensor", lambda e, c4=c4, kt=kt, po_=po_, qfb=qfb:
                                      e.matmul(
                                          po_[:, :], KF[c4][64:128, kt * 128:(kt + 1) * 128],
                                          qfb[64:128, :], start=True, stop=True,
                                          tile_position=(64, 0)), incs=[("t_pe", 1)])
                                qk_t = ("t_pe", kb.val("t_pe"))
                                kb.op("scalar", lambda e, pe_=pe_, c4=c4, eb=eb: e.activation(
                                    out=eb[:, :], in_=pe_[:, :], func=AF.Exp,
                                    bias=negC[:, 2 * c4:2 * c4 + 1], scale=0.125),
                                    waits=[qk_t, negc_v, expT_free[par]])
                                psA_free[par * 2] = ("a_sc", kb.val("a_sc"))
                                exp_e = ("a_sc", kb.val("a_sc"))
                                kb.op("tensor", lambda e, c4=c4, kt=kt, eb=eb, vap=vap:
                                      e.matmul(
                                          psO[0][0:65, :],
                                          bass.AP(vap.tensor,
                                                  vap.offset + kt * VST + 2 * c4 * 65,
                                                  [[vap.ap[0][0], 128], [1, 65]]),
                                          eb[:, :], start=(kt == 0), stop=(kt == KC - 1)),
                                      waits=[exp_e] + ([psO_free[0]] if kt == 0 else []),
                                      incs=[("t_pe", 1)])
                                ave_t = kb.val("t_pe")
                                kb.op("scalar", lambda e, po_=po_, c4=c4, eb=eb: e.activation(
                                    out=eb[:, :], in_=po_[:, :], func=AF.Exp,
                                    bias=negC[:, 2 * c4 + 1:2 * c4 + 2], scale=0.125),
                                    waits=[("t_pe", ave_t)])
                                psA_free[par * 2 + 1] = ("a_sc", kb.val("a_sc"))
                                exp_o = ("a_sc", kb.val("a_sc"))
                                kb.op("tensor", lambda e, c4=c4, kt=kt, eb=eb, vap=vap:
                                      e.matmul(
                                          psO[1][0:65, :],
                                          bass.AP(vap.tensor,
                                                  vap.offset + kt * VST + (2 * c4 + 1) * 65,
                                                  [[vap.ap[0][0], 128], [1, 65]]),
                                          eb[:, :], start=(kt == 0), stop=(kt == KC - 1)),
                                      waits=[exp_o] + ([psO_free[1]] if kt == 0 else []),
                                      incs=[("t_pe", 1)])
                                expT_free[par] = ("t_pe", kb.val("t_pe"))
                            av_t = ("t_pe", kb.val("t_pe"))
                            for j in range(2):
                                kb.op("vector", lambda e, j=j: e.reciprocal(
                                    rd_f[:, :], psO[j][64:65, :]),
                                    waits=[av_t, mr_w, att_dma])
                                kb.op("vector", lambda e, j=j: e.tensor_tensor(
                                    rdm_b[:, :], rd_f[:, :], mrow[j][:, :], ALU.mult))
                                vr = ("v_dv", kb.val("v_dv"))
                                mrow_v = vr
                                kb.op("tensor", lambda e: e.matmul(
                                    bc_ps[0:64, :], ones1_64[:, :], rdm_b[:, :],
                                    start=True, stop=True), waits=[vr, bc_free],
                                    incs=[("t_pe", 1)])
                                tbc = ("t_pe", kb.val("t_pe"))
                                kb.op("vector", lambda e: e.tensor_copy(
                                    bc_sb[:, :], bc_ps[0:64, :]), waits=[tbc])
                                bc_free = ("v_dv", kb.val("v_dv"))
                                kb.op("vector", lambda e, j=j: e.tensor_tensor(
                                    att_b[:, :], psO[j][0:64, :], bc_sb[:, :], ALU.mult))
                                psO_free[j] = ("v_dv", kb.val("v_dv"))
                                att_v = ("v_dv", kb.val("v_dv"))
                                kb.dma("sync", lambda e, c4=c4, qg=qg, j=j, p=p: e.dma_start(
                                    out=bass.AP(ATTD.ap().tensor,
                                                ((p * HPP + 2 * c4 + j) * 64) * half + qg * 512,
                                                [[half, 64], [1, 512]]),
                                    in_=att_b[:, :]), "h_att", waits=[att_v])
                                att_dma = ("h_att", kb.val("h_att"))
                            qf_used_pe[0] = kb.val("t_pe")
        st_att.close()
        att_all = [("h_att", kb.val("h_att")), ("t_pe", kb.val("t_pe")),
                   ("v_dv", kb.val("v_dv")), ("a_sc", kb.val("a_sc"))]

        with ExitStack() as stw:
            SBw = lambda n, shp, dt: stw.enter_context(nc.sbuf_tensor(n, shp, dt))
            PSw = lambda n, shp, dt: stw.enter_context(nc.psum_tensor(n, shp, dt))
            wo_sb = SBw("wo_sb", [128, DC, D], BF16)
            attf = [SBw(f"attf{i}", [128, DC, 512], BF16) for i in range(2)]
            xres = SBw("xres", [128, DC, 512], F32)
            res1 = SBw("res1", [128, DC, 512], F32)
            r1b = SBw("r1b", [128, 512], BF16)
            r1sq = SBw("r1sq", [128, 512], BF16)
            msb = SBw("msb", [128, 512], F32)
            m2sb = SBw("m2sb", [128, 512], F32)
            rstd = SBw("rstd", [128, 512], F32)
            evtmp = SBw("evtmp", [128, 512], F32)
            ln1c = SBw("ln1c", [128, 512], F32)
            ln1cb = SBw("ln1cb", [128, 512], BF16)
            wops = [PSw(f"wops{i}", [128, 512], F32) for i in range(8)]
            kb.dma("gpsimd", lambda e: e.dma_start(
                out=wo_sb[:, :, :],
                in_=bass.AP(woT.ap().tensor, 0, [[D, 128], [128 * D, DC], [1, D]])),
                "g_w", waits=att_all)
            gwo = ("g_w", kb.val("g_w"))
            wo_banks_free = {i: ("a_sc", 0) for i in range(8)}
            stat_t = ("t_pe", 0)
            ln_dma = ("h_ln", 0)
            attf_pe = {}
            xres_v = ("v_dv", 0)
            for qg in range(QG):
                ab = attf[qg % 2]
                w_af = [att_all[0]]
                if qg >= 2:
                    w_af.append(("t_pe", attf_pe[qg - 2]))
                kb.dma("sync", lambda e, qg=qg, ab=ab: e.dma_start(
                    out=ab[:, :, :],
                    in_=bass.AP(ATTD.ap().tensor, qg * 512,
                                [[half, 128], [128 * half, DC], [1, 512]])), "h_af",
                    waits=w_af)
                af_w = ("h_af", kb.val("h_af"))
                kb.dma("gpsimd", lambda e, qg=qg: e.dma_start(
                    out=xres[:, :, :],
                    in_=bass.AP(xT.ap().tensor, qg * 512,
                                [[L, 128], [128 * L, DC], [1, 512]])), "g_xr",
                    waits=[xres_v])
                xr_w = ("g_xr", kb.val("g_xr"))
                for do in range(DC):
                    w0 = [af_w, gwo, wo_banks_free[do]]
                    for di in range(DC):
                        kb.op("tensor", lambda e, do=do, di=di, ab=ab: e.matmul(
                            wops[do][:, :], wo_sb[:, di, do * 128:(do + 1) * 128],
                            ab[:, di, :], start=(di == 0), stop=(di == DC - 1)),
                            waits=(w0 if di == 0 else []),
                            incs=([("t_pe", 1)] if di == DC - 1 else []))
                attf_pe[qg] = kb.val("t_pe")
                wo_t = ("t_pe", kb.val("t_pe"))
                res_v = {}
                for do in range(DC):
                    kb.op("scalar", lambda e, do=do: e.activation(
                        out=evtmp[:, :], in_=wops[do][:, :],
                        func=AF.Identity, bias=bocol[:, do:do + 1]),
                        waits=[wo_t] + ([res_v[do - 1]] if do >= 1 else []))
                    ev_a = ("a_sc", kb.val("a_sc"))
                    wo_banks_free[do] = ev_a
                    kb.op("vector", lambda e, do=do: e.tensor_tensor(
                        res1[:, do, :], evtmp[:, :], xres[:, do, :], ALU.add),
                        waits=[ev_a, xr_w])
                    res_v[do] = ("v_dv", kb.val("v_dv"))
                xres_v = res_v[DC - 1]
                sum_ps, sq_ps = wops[0], wops[1]
                sum_t, sq_t = {}, {}
                for do in range(DC):
                    kb.op("vector", lambda e, do=do: e.tensor_copy(
                        r1b[:, :], res1[:, do, :]),
                        waits=[res_v[DC - 1], stat_t] +
                              ([("t_pe", sum_t[do - 1])] if do >= 1 else []))
                    vb_ = ("v_dv", kb.val("v_dv"))
                    kb.op("scalar", lambda e, do=do: e.activation(
                        out=r1sq[:, :], in_=res1[:, do, :], func=AF.Square),
                        waits=[stat_t] +
                              ([("t_pe", sq_t[do - 1])] if do >= 1 else []))
                    sb_ = ("a_sc", kb.val("a_sc"))
                    kb.op("tensor", lambda e, do=do: e.matmul(
                        sum_ps[:, :], allones[:, :], r1b[:, :],
                        start=(do == 0), stop=(do == DC - 1)),
                        waits=([vb_, cst, wo_banks_free[0]] if do == 0 else [vb_]),
                        incs=[("t_pe", 1)])
                    sum_t[do] = kb.val("t_pe")
                    kb.op("tensor", lambda e, do=do: e.matmul(
                        sq_ps[:, :], allones[:, :], r1sq[:, :],
                        start=(do == 0), stop=(do == DC - 1)),
                        waits=([sb_, wo_banks_free[1]] if do == 0 else [sb_]),
                        incs=[("t_pe", 1)])
                    sq_t[do] = kb.val("t_pe")
                stat_t = ("t_pe", kb.val("t_pe"))
                kb.op("vector", lambda e: e.tensor_scalar(
                    msb[:, :], sum_ps[:, :], 1.0 / D, None, ALU.mult), waits=[stat_t])
                kb.op("vector", lambda e: e.tensor_tensor(
                    m2sb[:, :], msb[:, :], msb[:, :], ALU.mult))
                kb.op("vector", lambda e: e.scalar_tensor_tensor(
                    m2sb[:, :], sq_ps[:, :], 1.0 / D, m2sb[:, :], ALU.mult,
                    ALU.subtract))
                wo_banks_free[0] = ("v_dv", kb.val("v_dv"))
                wo_banks_free[1] = ("v_dv", kb.val("v_dv"))
                vvar = ("v_dv", kb.val("v_dv"))
                kb.op("scalar", lambda e: e.activation(
                    out=rstd[:, :], in_=m2sb[:, :], func=AF.Ln, bias=1e-5),
                    waits=[vvar])
                kb.op("scalar", lambda e: e.activation(
                    out=rstd[:, :], in_=rstd[:, :], func=AF.Exp, scale=-0.5))
                rs_a = ("a_sc", kb.val("a_sc"))
                for do in range(DC):
                    kb.op("vector", lambda e, do=do: e.tensor_tensor(
                        ln1c[:, :], res1[:, do, :], msb[:, :], ALU.subtract),
                        waits=[rs_a, ln_dma])
                    kb.op("vector", lambda e: e.tensor_tensor(
                        ln1c[:, :], ln1c[:, :], rstd[:, :], ALU.mult))
                    kb.op("vector", lambda e: e.tensor_copy(ln1cb[:, :], ln1c[:, :]))
                    lnv = ("v_dv", kb.val("v_dv"))
                    kb.dma("sync", lambda e, qg=qg, do=do: e.dma_start(
                        out=bass.AP(LN1D.ap().tensor, (do * 128) * half + qg * 512,
                                    [[half, 128], [1, 512]]),
                        in_=ln1c[:, :]), "h_ln", waits=[lnv])
                    kb.dma("sync", lambda e, qg=qg, do=do: e.dma_start(
                        out=bass.AP(LN1B.ap().tensor, (do * 128) * half + qg * 512,
                                    [[half, 128], [1, 512]]),
                        in_=ln1cb[:, :]), "h_ln")
                    ln_dma = ("h_ln", kb.val("h_ln"))
        ln1_all = [("h_ln", kb.val("h_ln")), ("t_pe", kb.val("t_pe")),
                   ("v_dv", kb.val("v_dv")), ("a_sc", kb.val("a_sc"))]

        with ExitStack() as stf:
            SBf = lambda n, shp, dt: stf.enter_context(nc.sbuf_tensor(n, shp, dt))
            PSf = lambda n, shp, dt: stf.enter_context(nc.psum_tensor(n, shp, dt))
            wg_sb = SBf("wg_sb", [128, DC, HID], BF16)
            wu_sb = SBf("wu_sb", [128, DC, HID], BF16)
            wd_sb = SBf("wd_sb", [128, HC, D], BF16)
            xg = [SBf(f"xg{i}", [128, DC, 512], BF16) for i in range(1)]
            lf = SBf("lf", [128, DC, 512], F32)
            hF = SBf("hF", [128, HC, 512], BF16)
            gps = [PSf(f"gps{i}", [128, 512], F32) for i in range(2)]
            ups = [PSf(f"ups{i}", [128, 512], F32) for i in range(2)]
            dps = [PSf(f"dps{i}", [128, 512], F32) for i in range(4)]
            wcast = ("g_gp", 0)
            ffn_w0 = ln1_all + [("h_ln", kb.val("h_ln"))]
            st_ws = ExitStack()
            wstage = st_ws.enter_context(nc.sbuf_tensor("wstage", [128, HID // 4], I32))
            HH = HID // 4
            for (srcw, dst, nch) in [(wgT, wg_sb, DC), (wuT, wu_sb, DC)]:
                for dcq in range(nch):
                    for hf_ in range(4):
                        kb.dma("gpsimd", lambda e, srcw=srcw, dcq=dcq, hf_=hf_: e.dma_start(
                            out=wstage[:, :],
                            in_=bass.AP(srcw.ap().tensor, dcq * 128 * HID + hf_ * HH,
                                        [[HID, 128], [1, HH]])), "g_wf",
                            waits=[wcast] + (ffn_w0 if (dcq == 0 and hf_ == 0) else []))
                        wload = ("g_wf", kb.val("g_wf"))
                        kb.op("gpsimd", lambda e, dst=dst, dcq=dcq, hf_=hf_: e.tensor_copy(
                            dst[:, dcq, hf_ * HH:(hf_ + 1) * HH], wstage[:, :]),
                            incs=[("g_gp", 1)], waits=[wload])
                        wcast = ("g_gp", kb.val("g_gp"))
            DH = D // 2
            for hc in range(HC):
                for hf_ in range(2):
                    kb.dma("gpsimd", lambda e, hc=hc, hf_=hf_: e.dma_start(
                        out=wstage[:, 0:DH],
                        in_=bass.AP(wdT.ap().tensor, hc * 128 * D + hf_ * DH,
                                    [[D, 128], [1, DH]])), "g_wf",
                        waits=[wcast])
                    wload = ("g_wf", kb.val("g_wf"))
                    kb.op("gpsimd", lambda e, hc=hc, hf_=hf_: e.tensor_copy(
                        wd_sb[:, hc, hf_ * DH:(hf_ + 1) * DH], wstage[:, 0:DH]),
                        incs=[("g_gp", 1)], waits=[wload])
                    wcast = ("g_gp", kb.val("g_gp"))
            wf_all = [wcast]
            st_ws.close()
            gb = SBf("gb", [128, 512], BF16)
            ub = SBf("ub", [128, 512], BF16)
            r2b = SBf("r2b", [128, 512], BF16)
            r2sq = SBf("r2sq", [128, 512], BF16)
            msb2 = SBf("msb2", [128, 512], F32)
            m2sb2 = SBf("m2sb2", [128, 512], F32)
            rstd2 = SBf("rstd2", [128, 512], F32)
            yc = SBf("yc", [128, 512], F32)

            lf_v = ("v_dv", 0)
            xg_used = {}
            gps_free = {i: ("a_sc", 0) for i in range(2)}
            ups_free = {i: ("a_sc", 0) for i in range(2)}
            dps_free = {i: ("a_sc", 0) for i in range(4)}
            h_used = ("t_pe", 0)
            y_dma = ("h_y", 0)
            stat2_t = ("t_pe", 0)
            for qg in range(QG):
                xgb = xg[0]
                w_xg = [ln1_all[0]]
                if qg >= 1:
                    w_xg.append(("t_pe", xg_used[qg - 1]))
                kb.dma("sync", lambda e, qg=qg, xgb=xgb: e.dma_start(
                    out=xgb[:, :, :],
                    in_=bass.AP(LN1B.ap().tensor, qg * 512,
                                [[half, 128], [128 * half, DC], [1, 512]])), "h_xg",
                    waits=w_xg)
                xg_w = ("h_xg", kb.val("h_xg"))
                kb.dma("gpsimd", lambda e, qg=qg: e.dma_start(
                    out=lf[:, :, :],
                    in_=bass.AP(LN1D.ap().tensor, qg * 512,
                                [[half, 128], [128 * half, DC], [1, 512]])), "g_lf",
                    waits=[ln1_all[0], lf_v])
                lf_w = ("g_lf", kb.val("g_lf"))
                for hc in range(HC):
                    gbank = hc % 2
                    w_g = [xg_w, gps_free[gbank]]
                    if qg == 0 and hc < 2:
                        w_g = w_g + wf_all
                    for di in range(DC):
                        kb.op("tensor", lambda e, hc=hc, di=di, xgb=xgb, gbank=gbank:
                              e.matmul(
                                  gps[gbank][:, :], wg_sb[:, di, hc * 128:(hc + 1) * 128],
                                  xgb[:, di, :], start=(di == 0), stop=(di == DC - 1)),
                              waits=(w_g if di == 0 else []))
                    for di in range(DC):
                        kb.op("tensor", lambda e, hc=hc, di=di, xgb=xgb, gbank=gbank:
                              e.matmul(
                                  ups[gbank][:, :], wu_sb[:, di, hc * 128:(hc + 1) * 128],
                                  xgb[:, di, :], start=(di == 0), stop=(di == DC - 1)),
                              waits=([ups_free[gbank]] if di == 0 else []),
                              incs=([("t_pe", 1)] if di == DC - 1 else []))
                    gu_t = ("t_pe", kb.val("t_pe"))
                    kb.op("scalar", lambda e, gbank=gbank: e.activation(
                        out=gb[:, :], in_=gps[gbank][:, :], func=AF.Silu,
                        scale=1.0 / 127.0),
                        waits=[gu_t, h_used, ("v_dv", kb.val("v_dv")),
                               ("h_dbg", kb.val("h_dbg"))])
                    gps_free[gbank] = ("a_sc", kb.val("a_sc"))
                    kb.op("scalar", lambda e, gbank=gbank: e.activation(
                        out=ub[:, :], in_=ups[gbank][:, :], func=AF.Copy,
                        scale=1.0 / 127.0))
                    ups_free[gbank] = ("a_sc", kb.val("a_sc"))
                    ug_a = ("a_sc", kb.val("a_sc"))
                    kb.op("vector", lambda e, hc=hc: e.tensor_tensor(
                        hF[:, hc, :], gb[:, :], ub[:, :], ALU.mult),
                        waits=[ug_a, h_used])
                    if DBG and qg == 0:
                        hv_ = ("v_dv", kb.val("v_dv"))
                        kb.dma("sync", lambda e, hc=hc: e.dma_start(
                            out=bass.AP(DBGG.ap().tensor, hc * 128 * 512,
                                        [[512, 128], [1, 512]]),
                            in_=gb[:, :]), "h_dbg", waits=[hv_])
                        kb.dma("sync", lambda e, hc=hc: e.dma_start(
                            out=bass.AP(DBGU.ap().tensor, hc * 128 * 512,
                                        [[512, 128], [1, 512]]),
                            in_=ub[:, :]), "h_dbg")
                        kb.dma("sync", lambda e, hc=hc: e.dma_start(
                            out=bass.AP(DBGH.ap().tensor, hc * 128 * 512,
                                        [[512, 128], [1, 512]]),
                            in_=hF[:, hc, :]), "h_dbg")
                        dbg_dma = ("h_dbg", kb.val("h_dbg"))
                h_v = ("v_dv", kb.val("v_dv"))
                res_v = {}
                for do in range(DC):
                    bank = do % 4
                    for hc in range(HC):
                        kb.op("tensor", lambda e, do=do, hc=hc, bank=bank: e.matmul(
                            dps[bank][:, :], wd_sb[:, hc, do * 128:(do + 1) * 128],
                            hF[:, hc, :], start=(hc == 0), stop=(hc == HC - 1)),
                            waits=([h_v, dps_free[bank]] if hc == 0 else []),
                            incs=([("t_pe", 1)] if hc == HC - 1 else []))
                    d_t = ("t_pe", kb.val("t_pe"))
                    kb.op("scalar", lambda e, do=do, bank=bank: e.activation(
                        out=yc[:, :], in_=dps[bank][:, :], func=AF.Copy,
                        scale=1.0 / 127.0),
                        waits=[d_t] + ([res_v[do - 1]] if do >= 1 else []))
                    dev = ("a_sc", kb.val("a_sc"))
                    dps_free[bank] = dev
                    kb.op("vector", lambda e, do=do: e.tensor_tensor(
                        lf[:, do, :], yc[:, :], lf[:, do, :], ALU.add),
                        waits=[dev, lf_w])
                    res_v[do] = ("v_dv", kb.val("v_dv"))
                h_used = ("t_pe", kb.val("t_pe"))
                sum_ps, sq_ps = dps[0], dps[1]
                sum_t2, sq_t2 = {}, {}
                for do in range(DC):
                    kb.op("vector", lambda e, do=do: e.tensor_copy(
                        r2b[:, :], lf[:, do, :]),
                        waits=[res_v[DC - 1], stat2_t] +
                              ([("t_pe", sum_t2[do - 1])] if do >= 1 else []))
                    vb_ = ("v_dv", kb.val("v_dv"))
                    kb.op("scalar", lambda e, do=do: e.activation(
                        out=r2sq[:, :], in_=lf[:, do, :], func=AF.Square),
                        waits=[stat2_t] +
                              ([("t_pe", sq_t2[do - 1])] if do >= 1 else []))
                    sb_ = ("a_sc", kb.val("a_sc"))
                    kb.op("tensor", lambda e, do=do: e.matmul(
                        sum_ps[:, :], allones[:, :], r2b[:, :],
                        start=(do == 0), stop=(do == DC - 1)),
                        waits=([vb_, dps_free[0]] if do == 0 else [vb_]),
                        incs=[("t_pe", 1)])
                    sum_t2[do] = kb.val("t_pe")
                    kb.op("tensor", lambda e, do=do: e.matmul(
                        sq_ps[:, :], allones[:, :], r2sq[:, :],
                        start=(do == 0), stop=(do == DC - 1)),
                        waits=([sb_, dps_free[1]] if do == 0 else [sb_]),
                        incs=[("t_pe", 1)])
                    sq_t2[do] = kb.val("t_pe")
                stat2_t = ("t_pe", kb.val("t_pe"))
                kb.op("vector", lambda e: e.tensor_scalar(
                    msb2[:, :], sum_ps[:, :], 1.0 / D, None, ALU.mult), waits=[stat2_t])
                kb.op("vector", lambda e: e.tensor_tensor(
                    m2sb2[:, :], msb2[:, :], msb2[:, :], ALU.mult))
                kb.op("vector", lambda e: e.scalar_tensor_tensor(
                    m2sb2[:, :], sq_ps[:, :], 1.0 / D, m2sb2[:, :], ALU.mult,
                    ALU.subtract))
                dps_free[0] = ("v_dv", kb.val("v_dv"))
                dps_free[1] = ("v_dv", kb.val("v_dv"))
                vvar = ("v_dv", kb.val("v_dv"))
                kb.op("scalar", lambda e: e.activation(
                    out=rstd2[:, :], in_=m2sb2[:, :], func=AF.Ln, bias=1e-5),
                    waits=[vvar])
                kb.op("scalar", lambda e: e.activation(
                    out=rstd2[:, :], in_=rstd2[:, :], func=AF.Exp, scale=-0.5))
                rs_a = ("a_sc", kb.val("a_sc"))
                if DBG and qg == 0:
                    for do in range(DC):
                        kb.dma("sync", lambda e, do=do: e.dma_start(
                            out=bass.AP(DBGL.ap().tensor, do * 128 * 512,
                                        [[512, 128], [1, 512]]),
                            in_=lf[:, do, :]), "h_dbg", waits=[rs_a])
                    kb.dma("sync", lambda e: e.dma_start(
                        out=DBGM.ap(), in_=msb2[:, :]), "h_dbg")
                    kb.dma("sync", lambda e: e.dma_start(
                        out=DBGR.ap(), in_=rstd2[:, :]), "h_dbg")
                for do in range(DC):
                    kb.op("vector", lambda e, do=do: e.tensor_tensor(
                        yc[:, :], lf[:, do, :], msb2[:, :], ALU.subtract),
                        waits=[rs_a, y_dma] +
                              ([("h_dbg", kb.val("h_dbg"))] if (DBG and qg == 0) else []))
                    kb.op("vector", lambda e: e.tensor_tensor(
                        yc[:, :], yc[:, :], rstd2[:, :], ALU.mult))
                    yv = ("v_dv", kb.val("v_dv"))
                    kb.dma("sync", lambda e, qg=qg, do=do: e.dma_start(
                        out=bass.AP(yout.ap().tensor, (do * 128) * half + qg * 512,
                                    [[half, 128], [1, 512]]),
                        in_=yc[:, :]), "h_y", waits=[yv])
                    y_dma = ("h_y", kb.val("h_y"))
                lf_v = ("v_dv", kb.val("v_dv"))
                xg_used[qg] = kb.val("t_pe")
            y_total = kb.val("h_y")

        for eng in KB.ENGS:
            kb.wait(eng, [("h_y", y_total)])

        with nc.Block() as block:
            @block.sync
            def _(sync):
                for fn in kb.ops["sync"]:
                    fn(sync)

            @block.tensor
            def _(tensor):
                for fn in kb.ops["tensor"]:
                    fn(tensor)

            @block.vector
            def _(vector):
                for fn in kb.ops["vector"]:
                    fn(vector)

            @block.scalar
            def _(scalar):
                for fn in kb.ops["scalar"]:
                    fn(scalar)

            @block.gpsimd
            def _(gpsimd):
                for fn in kb.ops["gpsimd"]:
                    fn(gpsimd)

    return nc



def make_consts(c: Cfg):
    import ml_dtypes
    pairsel = np.zeros((128, 2), np.float32)
    pairsel[0:64, 0] = 1.0
    pairsel[64:128, 1] = 1.0
    gsel8 = np.zeros((64, 8), np.float32)
    for r in range(64):
        gsel8[r, r // 8] = 1.0
    gbc8 = np.zeros((8, 64), np.float32)
    for r in range(64):
        gbc8[r // 8, r] = 1.0
    ident8 = np.eye(8, dtype=np.float32)
    ones8_128 = np.ones((8, 128), np.float32)
    ones1_64 = np.ones((1, 64), ml_dtypes.bfloat16)
    ones2_128 = np.ones((2, 128), ml_dtypes.bfloat16)
    allones = np.ones((128, 128), ml_dtypes.bfloat16)
    return dict(PAIRSEL=pairsel, GSEL8=gsel8, GBC8=gbc8, IDENT8=ident8,
                ONES8_128=ones8_128, ONES1_64=ones1_64, ONES2_128=ones2_128,
                ALLONES=allones)


def stage_core(c: Cfg, core, x, Wq, bq, Wk, bk, Wv, bv, Wo, bo, wg, wu, wd, consts):
    b = core // 2
    h = core % 2
    if h == 0:
        perm = np.r_[0:c.half, c.half:c.L]
    else:
        perm = np.r_[c.half:c.L, 0:c.half]
    xl = x[b][perm]
    d = dict(
        xT=np.ascontiguousarray(xl.T),
        wqT=np.ascontiguousarray(Wq.T),
        wkT=np.ascontiguousarray(Wk.T),
        wvT=np.ascontiguousarray(Wv.T),
        woT=np.ascontiguousarray(Wo.T),
        bqc=np.ascontiguousarray(bq.reshape(c.DC, 128).T),
        bkc=np.ascontiguousarray(bk.reshape(c.DC, 128).T),
        boc=np.ascontiguousarray(bo.reshape(c.DC, 128).T),
        bvr=np.ascontiguousarray(bv.reshape(1, c.D)),
        wgT=np.ascontiguousarray(wg.T),
        wuT=np.ascontiguousarray(wu.T),
        wdT=np.ascontiguousarray(wd.T),
    )
    d.update(consts)
    return d


_CFG = full_cfg()
_NC_CACHE = []


def kernel(L0_attn_x, L0_attn_Wq, L0_attn_bq, L0_attn_Wk, L0_attn_bk,
           L0_attn_Wv, L0_attn_bv, L0_attn_Wo, L0_attn_bo,
           L0_ffn_w_gate, L0_ffn_w_up, L0_ffn_w_down):
    c = _CFG
    x = np.ascontiguousarray(np.asarray(L0_attn_x, np.float32))
    consts = make_consts(c)
    args = (x, np.asarray(L0_attn_Wq, np.float32), np.asarray(L0_attn_bq, np.float32),
            np.asarray(L0_attn_Wk, np.float32), np.asarray(L0_attn_bk, np.float32),
            np.asarray(L0_attn_Wv, np.float32), np.asarray(L0_attn_bv, np.float32),
            np.asarray(L0_attn_Wo, np.float32), np.asarray(L0_attn_bo, np.float32),
            np.asarray(L0_ffn_w_gate, np.int32), np.asarray(L0_ffn_w_up, np.int32),
            np.asarray(L0_ffn_w_down, np.int32))
    in_maps = [stage_core(c, core, *args, consts) for core in range(c.n_cores)]
    if not _NC_CACHE:
        _NC_CACHE.append(build_nc(c))
    res = run_bass_kernel_spmd(_NC_CACHE[0], in_maps, core_ids=list(range(c.n_cores)))
    B = 4
    out = np.empty((B, c.L, c.D), np.float32)
    for b in range(B):
        out[b, 0:c.half] = res.results[2 * b]["yout"].T
        out[b, c.half:c.L] = res.results[2 * b + 1]["yout"].T
    return out
